# version 01 (final)
# baseline (speedup 1.0000x reference)
import contextlib
import numpy as np
import concourse.bass as bass
import concourse.mybir as mybir
from concourse.bass_utils import run_bass_kernel_spmd
from concourse.bass_core_rust import IndirectOffsetOnAxis

F32 = mybir.dt.float32
BF16 = mybir.dt.bfloat16
I32 = mybir.dt.int32
AF = mybir.ActivationFunctionType
OP = mybir.AluOpType
AX = mybir.AxisListType.X

NCORE = 8
B, T, D = 8, 4096, 1024


class Prog:

    def __init__(self):
        self.nc = bass.Bass("TRN2", target_bir_lowering=False)
        self.plan = []
        self.stack = contextlib.ExitStack()
        self._consts = {}
        self._cpool = None

    def cst(self, val, dt=F32):
        key = (dt, float(val))
        if key not in self._consts:
            if self._cpool is None:
                self._cpool = self.sb("cpool", [128, 32], F32)
            col = len(self._consts)
            ap = self._cpool[:, col:col + 1]
            self.v('memset', ap, float(val))
            self.nc.const_aps.register(dt, float(val), ap)
            self._consts[key] = ap
        return self._consts[key]

    def sb(self, name, shape, dt=F32):
        return self.stack.enter_context(self.nc.sbuf_tensor(name, shape, dt))

    def ps(self, name, shape, dt=F32):
        return self.stack.enter_context(self.nc.psum_tensor(name, shape, dt))

    def inp(self, name, shape, dt=F32):
        return self.nc.dram_tensor(name, shape, dt, kind="ExternalInput").ap()

    def outp(self, name, shape, dt=F32):
        return self.nc.dram_tensor(name, shape, dt, kind="ExternalOutput").ap()

    def dma(self, dst, src):
        self.plan.append(('sync', 16,
                          lambda d=dst, s=src: self.nc.sync.dma_start(out=d, in_=s)))

    def gdma(self, dst, src):
        self.plan.append(('gpsimd', 16,
                          lambda d=dst, s=src: self.nc.gpsimd.dma_start(out=d, in_=s)))

    def gather(self, dst, src, idx):
        self.plan.append(('gpsimd', 16,
                          lambda d=dst, s=src, i=idx: self.nc.gpsimd.indirect_dma_start(
                              d, None, s, IndirectOffsetOnAxis(i, 0))))

    def mm(self, out, lhsT, rhs, start=True, stop=True):
        self.plan.append(('tensor', 1,
                          lambda o=out, l=lhsT, r=rhs, st=start, sp=stop:
                          self.nc.tensor.matmul(o, l, r, start=st, stop=sp)))

    def tp(self, out, in_, ident):
        self.plan.append(('tensor', 1,
                          lambda o=out, i=in_, d=ident: self.nc.tensor.transpose(o, i, d)))

    def v(self, meth, *a, **k):
        self.plan.append(('vector', 1,
                          lambda m=meth, aa=a, kk=k: getattr(self.nc.vector, m)(*aa, **kk)))

    def s(self, meth, *a, **k):
        self.plan.append(('scalar', 1,
                          lambda m=meth, aa=a, kk=k: getattr(self.nc.scalar, m)(*aa, **kk)))

    def act(self, out, in_, func, bias=0.0, scale=1.0):
        if isinstance(bias, float) and bias not in (0.0, 1.0, -1.0):
            self.cst(bias)
        if isinstance(scale, float) and scale not in (0.0, 1.0, -1.0):
            self.cst(scale)
        self.s('activation', out, in_, func, bias, scale)

    def run(self):
        totals = []
        t = 0
        for e, inc, fn in self.plan:
            totals.append(t)
            t += inc
        grand = t
        with self.nc.semaphore("S") as S, self.nc.Block() as block:
            def mk(en):
                def body(_eng):
                    w = getattr(self.nc, en)
                    for (e, inc, fn), before in zip(self.plan, totals):
                        if e != en:
                            continue
                        w.wait_ge(S, before)
                        inst = fn()
                        inst.then_inc(S, inc)
                    w.wait_ge(S, grand)
                return body
            block.tensor(mk('tensor'))
            block.vector(mk('vector'))
            block.scalar(mk('scalar'))
            block.gpsimd(mk('gpsimd'))
            block.sync(mk('sync'))
        self.stack.close()
        return self.nc


def emit_ln(P, out_ap, in_ap, tmp, red, eps, n, gam=None, bet=None):
    P.v('reduce_sum', red[:, 0:1], in_ap, AX)
    P.v('tensor_scalar_mul', red[:, 0:1], red[:, 0:1], 1.0 / n)
    P.v('tensor_scalar', out_ap, in_ap, red[:, 0:1], None, OP.subtract)
    P.act(tmp, out_ap, AF.Square)
    P.v('reduce_sum', red[:, 1:2], tmp, AX)
    P.act(red[:, 1:2], red[:, 1:2], AF.Sqrt, eps, 1.0 / n)
    P.v('reciprocal', red[:, 1:2], red[:, 1:2])
    P.v('tensor_scalar_mul', out_ap, out_ap, red[:, 1:2])
    if gam is not None:
        P.v('tensor_mul', out_ap, out_ap, gam)
        P.v('tensor_add', out_ap, out_ap, bet)



def build_stageA():
    P = Prog()
    x = P.inp("x", [T, D])
    W = P.inp("W", [D, 2 * D])
    bcol = P.inp("bcol", [128, 16])
    gam = P.inp("gam", [128, D])
    bet = P.inp("bet", [128, D])
    ident = P.inp("ident", [128, 128])
    y = P.outp("y", [T, D])

    idn = P.sb("idn", [128, 128])
    P.dma(idn[:], ident)
    bc = P.sb("bc", [128, 16])
    P.dma(bc[:], bcol)
    gm = P.sb("gm", [128, D]); P.dma(gm[:], gam)
    bt = P.sb("bt", [128, D]); P.dma(bt[:], bet)

    TT = 2048
    xT = P.sb("xT", [128, 8 * TT], BF16)
    hs = P.sb("hs", [128, 8 * TT])
    wst = P.sb("wst", [128, 128])
    wpc = P.sb("wpc", [128, 16 * 128], BF16)
    xsb = P.sb("xsb", [128, D])
    ysb = P.sb("ysb", [128, D])
    Hb = P.sb("Hb", [128, TT])
    Kb = P.sb("Kb", [128, TT])
    t1 = P.sb("t1", [128, TT])
    t2 = P.sb("t2", [128, TT])
    t3 = P.sb("t3", [128, TT])
    msk = P.sb("msk", [128, TT], I32)
    tmp = P.sb("tmpln", [128, D])
    red = P.sb("redln", [128, 2])
    carry = P.sb("carry", [128, 8])
    p512 = P.ps("p512", [128, 512])
    ptp = P.ps("ptp", [128, 128])

    P.v('memset', carry[:], 0.0)

    for half in range(2):
        t0 = half * TT
        for i in range(16):
            P.dma(xsb[:], x[t0 + i * 128: t0 + (i + 1) * 128, :])
            for dc in range(8):
                P.tp(ptp[:], xsb[:, dc * 128:(dc + 1) * 128], idn[:])
                P.v('tensor_copy', xT[:, dc * TT + i * 128: dc * TT + (i + 1) * 128], ptp[:])
        for pc in range(8):
            for j, ec in enumerate((pc, pc + 8)):
                for dc in range(8):
                    P.dma(wst[:], W[dc * 128:(dc + 1) * 128, ec * 128:(ec + 1) * 128])
                    P.v('tensor_copy', wpc[:, (j * 8 + dc) * 128:(j * 8 + dc + 1) * 128], wst[:])
            for j, (ec, dst) in enumerate(((pc, Hb), (pc + 8, Kb))):
                for tc in range(TT // 512):
                    for dc in range(8):
                        P.mm(p512[:], wpc[:, (j * 8 + dc) * 128:(j * 8 + dc + 1) * 128],
                             xT[:, dc * TT + tc * 512: dc * TT + (tc + 1) * 512],
                             start=(dc == 0), stop=(dc == 7))
                    P.v('tensor_copy', dst[:, tc * 512:(tc + 1) * 512], p512[:])
                P.v('tensor_scalar_add', dst[:], dst[:], bc[:, ec:ec + 1])
            P.act(t1[:], Kb[:], AF.Sigmoid)
            P.act(t2[:], Kb[:], AF.Sigmoid, 0.0, -1.0)
            P.act(Kb[:], Hb[:], AF.Sigmoid)
            P.v('tensor_scalar', msk[:], Hb[:], 0.0, None, OP.is_ge)
            P.v('tensor_scalar_add', Hb[:], Hb[:], 0.5)
            P.v('select', Kb[:], msk[:], Hb[:], Kb[:])
            P.v('tensor_mul', t3[:], t1[:], Kb[:])
            for c in range(TT // 512):
                init = carry[:, pc:pc + 1] if c == 0 else hs[:, pc * TT + c * 512 - 1: pc * TT + c * 512]
                P.v('tensor_tensor_scan', hs[:, pc * TT + c * 512: pc * TT + (c + 1) * 512],
                    t2[:, c * 512:(c + 1) * 512], t3[:, c * 512:(c + 1) * 512],
                    init, OP.mult, OP.add)
            P.v('tensor_copy', carry[:, pc:pc + 1], hs[:, pc * TT + TT - 1: pc * TT + TT])
        for i in range(16):
            P.dma(xsb[:], x[t0 + i * 128: t0 + (i + 1) * 128, :])
            for dc in range(8):
                P.tp(ptp[:], hs[:, dc * TT + i * 128: dc * TT + (i + 1) * 128], idn[:])
                P.v('tensor_copy', ysb[:, dc * 128:(dc + 1) * 128], ptp[:])
            P.v('tensor_add', ysb[:], ysb[:], xsb[:])
            emit_ln(P, ysb[:], ysb[:], tmp[:], red[:], 1e-6, D, gm[:], bt[:])
            P.v('tensor_add', ysb[:], ysb[:], xsb[:])
            emit_ln(P, ysb[:], ysb[:], tmp[:], red[:], 1e-5, D)
            P.dma(y[t0 + i * 128: t0 + (i + 1) * 128, :], ysb[:])
    return P.run()


def run_stageA(inputs, x_full):
    nc = build_stageA()
    W = np.ascontiguousarray(inputs["L0_attn_W"], dtype=np.float32)
    bcol = np.ascontiguousarray(inputs["L0_attn_b"].reshape(16, 128).T, dtype=np.float32)
    gam = np.ascontiguousarray(np.broadcast_to(inputs["L0_attn_gamma"], (128, D)), dtype=np.float32)
    bet = np.ascontiguousarray(np.broadcast_to(inputs["L0_attn_beta"], (128, D)), dtype=np.float32)
    ident = np.eye(128, dtype=np.float32)
    in_maps = []
    for b in range(NCORE):
        in_maps.append({"x": np.ascontiguousarray(x_full[b], dtype=np.float32),
                        "W": W, "bcol": bcol, "gam": gam, "bet": bet, "ident": ident})
    res = run_bass_kernel_spmd(nc, in_maps, core_ids=list(range(NCORE)))
    return np.stack([r["y"] for r in res.results], axis=0)





def build_stageB1():
    P = Prog()
    x = P.inp("x", [T, D])
    gwl = P.inp("gwl", [128, 8 * 32])
    ident = P.inp("ident", [128, 128])
    Lstrict = P.inp("Lstrict", [128, 128])
    ones1 = P.inp("ones1", [1, 128])
    ones128 = P.inp("ones128", [128, 1])
    iota32 = P.inp("iota32", [128, 32])
    iota1000 = P.inp("iota1000", [128, 32])
    gates_o = P.outp("gates", [T, 4])
    idx_o = P.outp("idx", [T, 4])
    pos_o = P.outp("pos", [T, 4])
    counts_o = P.outp("counts", [1, 32])

    idn = P.sb("idn", [128, 128]); P.dma(idn[:], ident)
    Ls = P.sb("Ls", [128, 128]); P.dma(Ls[:], Lstrict)
    o1 = P.sb("o1", [1, 128]); P.dma(o1[:], ones1)
    o128 = P.sb("o128", [128, 1]); P.dma(o128[:], ones128)
    io32 = P.sb("io32", [128, 32]); P.dma(io32[:], iota32)
    io1k = P.sb("io1k", [128, 32]); P.dma(io1k[:], iota1000)
    gw = P.sb("gw", [128, 8 * 32]); P.dma(gw[:], gwl)
    gwb = P.sb("gwb", [128, 8 * 32], BF16); P.v('tensor_copy', gwb[:], gw[:])

    xsb = P.sb("xsb", [128, D])
    xT = P.sb("xTb", [128, 8 * 128], BF16)
    lg = P.sb("lg", [128, 32])
    Isb = P.sb("Isb", [128, 32])
    eqks = P.sb("eqks", [128, 4 * 32])
    mv = P.sb("mv", [128, 4])
    wv = P.sb("wv", [128, 4])
    t32 = P.sb("t32", [128, 32])
    r1 = P.sb("r1", [128, 2])
    rank = P.sb("rank", [128, 32])
    carry = P.sb("carryc", [1, 32])
    csb = P.sb("csb", [1, 32])
    outt = P.sb("outt", [128, 12])
    ptp = P.ps("ptpB", [128, 128])
    pl = P.ps("plB", [128, 32])
    prank = P.ps("prankB", [128, 32])
    pcnt = P.ps("pcntB", [1, 32])

    P.v('memset', carry[:], 0.0)
    nt = T // 128
    for i in range(nt):
        P.dma(xsb[:], x[i * 128:(i + 1) * 128, :])
        for dc in range(8):
            P.tp(ptp[:], xsb[:, dc * 128:(dc + 1) * 128], idn[:])
            P.v('tensor_copy', xT[:, dc * 128:(dc + 1) * 128], ptp[:])
            P.mm(pl[:], xT[:, dc * 128:(dc + 1) * 128], gwb[:, dc * 32:(dc + 1) * 32],
                 start=(dc == 0), stop=(dc == 7))
        P.v('tensor_copy', lg[:], pl[:])
        P.v('memset', Isb[:], 0.0)
        for k in range(4):
            P.v('reduce_max', mv[:, k:k + 1], lg[:], AX)
            P.v('tensor_scalar', t32[:], lg[:], mv[:, k:k + 1], None, OP.is_equal)
            P.v('tensor_scalar_mul', t32[:], t32[:], -1000.0)
            P.v('tensor_add', t32[:], t32[:], io1k[:])
            P.v('tensor_reduce', outt[:, 4 + k:5 + k], t32[:], AX, OP.min)
            eqk = eqks[:, k * 32:(k + 1) * 32]
            P.v('tensor_scalar', eqk, io32[:], outt[:, 4 + k:5 + k], None, OP.is_equal)
            P.v('tensor_add', Isb[:], Isb[:], eqk)
            P.v('tensor_scalar_mul', t32[:], eqk, -1e30)
            P.v('tensor_add', lg[:], lg[:], t32[:])
        P.v('tensor_scalar', wv[:], mv[:], mv[:, 0:1], None, OP.subtract)
        P.act(wv[:], wv[:], AF.Exp)
        P.v('reduce_sum', r1[:, 0:1], wv[:], AX)
        P.v('reciprocal', r1[:, 0:1], r1[:, 0:1])
        P.v('tensor_scalar_mul', outt[:, 0:4], wv[:], r1[:, 0:1])
        P.mm(prank[:], Ls[:], Isb[:], start=True, stop=False)
        P.mm(prank[:], o1[:], carry[:], start=False, stop=True)
        P.v('tensor_copy', rank[:], prank[:])
        P.mm(pcnt[:], o128[:], Isb[:], start=(i == 0), stop=True)
        P.v('tensor_copy', carry[:], pcnt[:])
        for k in range(4):
            P.v('tensor_mul', t32[:], rank[:], eqks[:, k * 32:(k + 1) * 32])
            P.v('reduce_sum', outt[:, 8 + k:9 + k], t32[:], AX)
        P.dma(gates_o[i * 128:(i + 1) * 128, :], outt[:, 0:4])
        P.dma(idx_o[i * 128:(i + 1) * 128, :], outt[:, 4:8])
        P.dma(pos_o[i * 128:(i + 1) * 128, :], outt[:, 8:12])
    P.v('tensor_copy', csb[:], pcnt[:])
    P.dma(counts_o, csb[:])
    return P.run()


def _emit_offsets(P, counts, idn_h, o1, e_sel=None, bsel=None):
    cnt = P.sb("cntS", [8, 32]); P.dma(cnt[:], counts)
    ctT = P.sb("ctT", [32, 8])
    zt = P.sb("ztS", [32, 8]); P.v('memset', zt[:], 0.0)
    ptp8 = P.ps("ptp8", [32, 8])
    P.tp(ptp8[:], cnt[:], idn_h[0:8, 0:8])
    P.v('tensor_copy', ctT[:], ptp8[:])
    inc = P.sb("incS", [32, 8])
    P.v('tensor_tensor_scan', inc[:], ctT[:], zt[:], 0.0, OP.add, OP.add)
    P.v('tensor_sub', inc[:], inc[:], ctT[:])
    offrow = P.sb("offrow", [1, 32])
    if bsel is not None:
        bs = P.sb("bsS", [32, 8]); P.dma(bs[:], bsel)
        P.v('tensor_mul', zt[:], inc[:], bs[:])
        offcol = P.sb("offcol", [32, 1])
        P.v('reduce_sum', offcol[:], zt[:], AX)
        P.gdma(offrow[:], offcol[:])
    else:
        for j, (e, b) in enumerate(e_sel):
            P.gdma(offrow[:, j:j + 1], inc[e:e + 1, b:b + 1])
    pofb = P.ps("pofb", [128, 32])
    P.mm(pofb[:], o1[:], offrow[:], start=True, stop=True)
    offb = P.sb("offb", [128, 32])
    P.v('tensor_copy', offb[:], pofb[:])
    return offb



def build_stageB2(e0):
    P = Prog()
    TK = B * T
    x1 = P.inp("x1", [TK, D])
    idx = P.inp("idx", [TK, 4])
    pos = P.inp("pos", [TK, 4])
    counts = P.inp("counts", [8, 32])
    wgu = P.inp("wgu", [4, D, D])
    wdn = P.inp("wdn", [4, 512, D])
    tokid = P.inp("tokid", [TK, 1])
    ident = P.inp("ident", [128, 128])
    ones1 = P.inp("ones1", [1, 128])
    iota1024 = P.inp("iota1024", [128, 1024])
    EO = P.outp("EO", [4096, D])

    idn = P.sb("idn", [128, 128]); P.dma(idn[:], ident)
    idnb = P.sb("idnb", [128, 128], BF16); P.v('tensor_copy', idnb[:], idn[:])
    o1 = P.sb("o1", [1, 128]); P.dma(o1[:], ones1)
    io1024 = P.sb("io1024", [128, 1024]); P.dma(io1024[:], iota1024)
    offb = _emit_offsets(P, counts, idn, o1[:], e_sel=[(e0 + el, b) for el in range(4) for b in range(8)])

    idxt = P.sb("idxt", [128, 4])
    post = P.sb("post", [128, 4])
    tok = P.sb("tok", [128, 1])
    t4 = P.sb("t4", [128, 4])
    c1 = P.sb("c1", [128, 4])
    posu = P.sb("posu", [128, 2])
    Msb = P.sb("Msb", [128, 1024])
    tidxf = P.sb("tidxf", [128, 32])
    tidxi = P.sb("tidxi", [128, 32], I32)
    ptidx = P.ps("ptidx", [128, 32])
    ptp = P.ps("ptpB2", [128, 128])

    nt = TK // 128
    for i in range(nt):
        bb = i // (T // 128)
        P.dma(idxt[:], idx[i * 128:(i + 1) * 128, :])
        P.dma(post[:], pos[i * 128:(i + 1) * 128, :])
        P.dma(tok[:], tokid[i * 128:(i + 1) * 128, :])
        for el in range(4):
            eg = float(e0 + el)
            P.v('tensor_scalar', t4[:], idxt[:], eg, None, OP.is_equal)
            P.v('tensor_mul', c1[:], t4[:], post[:])
            P.v('reduce_sum', posu[:, 0:1], c1[:], AX)
            P.v('reduce_sum', posu[:, 1:2], t4[:], AX)
            P.v('tensor_scalar_add', posu[:, 0:1], posu[:, 0:1], offb[:, el * 8 + bb: el * 8 + bb + 1])
            P.v('tensor_scalar_mul', posu[:, 0:1], posu[:, 0:1], posu[:, 1:2])
            P.v('tensor_scalar_mul', posu[:, 1:2], posu[:, 1:2], 5.0)
            P.v('tensor_scalar_add', posu[:, 1:2], posu[:, 1:2], -5.0)
            P.v('tensor_add', posu[:, 0:1], posu[:, 0:1], posu[:, 1:2])
            P.v('tensor_scalar', Msb[:], io1024[:], posu[:, 0:1], None, OP.is_equal)
            for cc in range(8):
                P.mm(ptidx[:, el * 8 + cc: el * 8 + cc + 1],
                     Msb[:, cc * 128:(cc + 1) * 128], tok[:],
                     start=(i == 0), stop=(i == nt - 1))
    P.v('tensor_copy', tidxf[:], ptidx[:])
    P.v('tensor_copy', tidxi[:], tidxf[:])

    wguf = P.sb("wguf", [128, 8 * 1024])
    wgub = P.sb("wgub", [128, 8 * 1024], BF16)
    wdnf = P.sb("wdnf", [128, 4 * 1024])
    wdnb = P.sb("wdnb", [128, 4 * 1024], BF16)
    xg = P.sb("xg", [128, 1024])
    xgT = P.sb("xgT", [128, 8 * 128], BF16)
    ssb = P.sb("ssb", [128, 512])
    mb = P.sb("mbB2", [128, 512], BF16)
    mT = P.sb("mT", [128, 4 * 128], BF16)
    eosb = P.sb("eosb", [128, 1024])
    pgu = P.ps("pguB2", [128, 1024])
    pdn = P.ps("pdnB2", [128, 1024])
    for el in range(4):
        for dc in range(8):
            P.dma(wguf[:, dc * 1024:(dc + 1) * 1024], wgu[el, dc * 128:(dc + 1) * 128, :])
        P.v('tensor_copy', wgub[:], wguf[:])
        for ic in range(4):
            P.dma(wdnf[:, ic * 1024:(ic + 1) * 1024], wdn[el, ic * 128:(ic + 1) * 128, :])
        P.v('tensor_copy', wdnb[:], wdnf[:])
        for cc in range(8):
            P.gather(xg[:], x1, tidxi[:, el * 8 + cc: el * 8 + cc + 1])
            for dc in range(8):
                P.tp(ptp[:], xg[:, dc * 128:(dc + 1) * 128], idn[:])
                P.v('tensor_copy', xgT[:, dc * 128:(dc + 1) * 128], ptp[:])
            for h in range(2):
                for dc in range(8):
                    P.mm(pgu[:, h * 512:(h + 1) * 512], xgT[:, dc * 128:(dc + 1) * 128],
                         wgub[:, dc * 1024 + h * 512: dc * 1024 + (h + 1) * 512],
                         start=(dc == 0), stop=(dc == 7))
            P.act(ssb[:], pgu[:, 0:512], AF.Silu)
            P.v('tensor_mul', ssb[:], ssb[:], pgu[:, 512:1024])
            for ic in range(4):
                P.tp(ptp[:], ssb[:, ic * 128:(ic + 1) * 128], idn[:])
                P.v('tensor_copy', mT[:, ic * 128:(ic + 1) * 128], ptp[:])
            for h in range(2):
                for ic in range(4):
                    P.mm(pdn[:, h * 512:(h + 1) * 512], mT[:, ic * 128:(ic + 1) * 128],
                         wdnb[:, ic * 1024 + h * 512: ic * 1024 + (h + 1) * 512],
                         start=(ic == 0), stop=(ic == 3))
            P.v('tensor_copy', eosb[:], pdn[:])
            P.dma(EO[el * 1024 + cc * 128: el * 1024 + (cc + 1) * 128, :], eosb[:])
    return P.run()



def build_stageB3():
    P = Prog()
    TK = B * T
    x = P.inp("x", [T, D])
    EOf = P.inp("EOf", [TK, D])
    idx = P.inp("idx", [T, 4])
    pos = P.inp("pos", [T, 4])
    gates = P.inp("gates", [T, 4])
    counts = P.inp("counts", [8, 32])
    sgu = P.inp("sgu", [D, D])
    sdn = P.inp("sdn", [512, D])
    ident = P.inp("ident", [128, 128])
    ones1 = P.inp("ones1", [1, 128])
    iota32 = P.inp("iota32", [128, 32])
    bsel = P.inp("bsel", [32, 8])
    y = P.outp("y", [T, D])

    idn = P.sb("idn", [128, 128]); P.dma(idn[:], ident)
    idnb = P.sb("idnb", [128, 128], BF16); P.v('tensor_copy', idnb[:], idn[:])
    o1 = P.sb("o1", [1, 128]); P.dma(o1[:], ones1)
    io32 = P.sb("io32", [128, 32]); P.dma(io32[:], iota32)
    offb = _emit_offsets(P, counts, idn, o1[:], bsel=bsel)

    sguf = P.sb("sguf", [128, 8 * 1024])
    sgub = P.sb("sgub", [128, 8 * 1024], BF16)
    sdnf = P.sb("sdnf", [128, 4 * 1024])
    sdnb = P.sb("sdnb", [128, 4 * 1024], BF16)
    for dc in range(8):
        P.dma(sguf[:, dc * 1024:(dc + 1) * 1024], sgu[dc * 128:(dc + 1) * 128, :])
    P.v('tensor_copy', sgub[:], sguf[:])
    for ic in range(4):
        P.dma(sdnf[:, ic * 1024:(ic + 1) * 1024], sdn[ic * 128:(ic + 1) * 128, :])
    P.v('tensor_copy', sdnb[:], sdnf[:])

    xsb = P.sb("xsb", [128, D])
    xT = P.sb("xTb", [128, 8 * 128], BF16)
    idxt = P.sb("idxt", [128, 4])
    post = P.sb("post", [128, 4])
    gt = P.sb("gt", [128, 4])
    t32 = P.sb("t32", [128, 32])
    r1 = P.sb("r1B", [128, 8])
    addri = P.sb("addri", [128, 1], I32)
    row = P.sb("rowB", [128, 1024])
    routed = P.sb("routed", [128, 1024])
    ssb = P.sb("ssbB", [128, 512])
    mb = P.sb("mbB", [128, 512], BF16)
    mT = P.sb("mTB", [128, 4 * 128], BF16)
    tmp = P.sb("tmpB", [128, D])
    red = P.sb("redB", [128, 2])
    ptp = P.ps("ptpB3", [128, 128])
    pgu = P.ps("pguB3", [128, 1024])
    pdn = P.ps("pdnB3", [128, 1024])

    nt = T // 128
    for i in range(nt):
        P.dma(xsb[:], x[i * 128:(i + 1) * 128, :])
        P.dma(idxt[:], idx[i * 128:(i + 1) * 128, :])
        P.dma(post[:], pos[i * 128:(i + 1) * 128, :])
        P.dma(gt[:], gates[i * 128:(i + 1) * 128, :])
        for k in range(4):
            P.v('tensor_scalar', t32[:], io32[:], idxt[:, k:k + 1], None, OP.is_equal)
            P.v('tensor_mul', t32[:], t32[:], offb[:])
            P.v('reduce_sum', r1[:, 0:1], t32[:], AX)
            P.v('tensor_add', r1[:, 0:1], r1[:, 0:1], post[:, k:k + 1])
            P.v('tensor_scalar', r1[:, 1:2], r1[:, 0:1], 1024.0, None, OP.is_lt)
            P.v('tensor_mul', r1[:, 1:2], r1[:, 1:2], gt[:, k:k + 1])
            P.v('tensor_scalar_min', r1[:, 0:1], r1[:, 0:1], 1023.0)
            P.v('tensor_scalar_mul', r1[:, 2:3], idxt[:, k:k + 1], 1024.0)
            P.v('tensor_add', r1[:, 0:1], r1[:, 0:1], r1[:, 2:3])
            P.v('tensor_copy', addri[:], r1[:, 0:1])
            P.gather(row[:], EOf, addri[:])
            if k == 0:
                P.v('tensor_scalar_mul', routed[:], row[:], r1[:, 1:2])
            else:
                P.v('tensor_scalar_mul', row[:], row[:], r1[:, 1:2])
                P.v('tensor_add', routed[:], routed[:], row[:])
        for dc in range(8):
            P.tp(ptp[:], xsb[:, dc * 128:(dc + 1) * 128], idn[:])
            P.v('tensor_copy', xT[:, dc * 128:(dc + 1) * 128], ptp[:])
        for h in range(2):
            for dc in range(8):
                P.mm(pgu[:, h * 512:(h + 1) * 512], xT[:, dc * 128:(dc + 1) * 128],
                     sgub[:, dc * 1024 + h * 512: dc * 1024 + (h + 1) * 512],
                     start=(dc == 0), stop=(dc == 7))
        P.act(ssb[:], pgu[:, 0:512], AF.Silu)
        P.v('tensor_mul', ssb[:], ssb[:], pgu[:, 512:1024])
        P.v('tensor_copy', mb[:], ssb[:])
        for ic in range(4):
            P.tp(ptp[:], ssb[:, ic * 128:(ic + 1) * 128], idn[:])
            P.v('tensor_copy', mT[:, ic * 128:(ic + 1) * 128], ptp[:])
        for h in range(2):
            for ic in range(4):
                P.mm(pdn[:, h * 512:(h + 1) * 512], mT[:, ic * 128:(ic + 1) * 128],
                     sdnb[:, ic * 1024 + h * 512: ic * 1024 + (h + 1) * 512],
                     start=(ic == 0), stop=(ic == 3))
        P.v('tensor_add', routed[:], routed[:], xsb[:])
        P.v('tensor_add', routed[:], routed[:], pdn[:])
        emit_ln(P, routed[:], routed[:], tmp[:], red[:], 1e-5, D)
        P.dma(y[i * 128:(i + 1) * 128, :], routed[:])
    return P.run()


def run_stageB(inputs, x1_full):
    TK = B * T
    ident = np.eye(128, dtype=np.float32)
    ones1 = np.ones((1, 128), dtype=np.float32)
    ones128 = np.ones((128, 1), dtype=np.float32)
    iota32 = np.broadcast_to(np.arange(32, dtype=np.float32), (128, 32)).copy()
    iota1000 = iota32 + 1000.0
    iota1024 = np.broadcast_to(np.arange(1024, dtype=np.float32), (128, 1024)).copy()
    Lstrict = np.triu(np.ones((128, 128), dtype=np.float32), 1)
    gwl = np.ascontiguousarray(
        inputs["L0_ffn_gate_w"].T.reshape(8, 128, 32).transpose(1, 0, 2).reshape(128, 256),
        dtype=np.float32)

    nc1 = build_stageB1()
    in_maps = [{"x": np.ascontiguousarray(x1_full[b], dtype=np.float32), "gwl": gwl,
                "ident": ident, "Lstrict": Lstrict, "ones1": ones1, "ones128": ones128,
                "iota32": iota32, "iota1000": iota1000} for b in range(NCORE)]
    print("B1 launch", flush=True)
    res1 = run_bass_kernel_spmd(nc1, in_maps, core_ids=list(range(NCORE)))
    print("B1 done", flush=True)
    gates = np.concatenate([r["gates"] for r in res1.results], 0)
    idxa = np.concatenate([r["idx"] for r in res1.results], 0)
    posa = np.concatenate([r["pos"] for r in res1.results], 0)
    counts = np.concatenate([r["counts"] for r in res1.results], 0)

    x1f = np.ascontiguousarray(x1_full.reshape(TK, D), dtype=np.float32)
    tokid = np.arange(TK, dtype=np.float32).reshape(TK, 1)
    wgu_all = np.ascontiguousarray(inputs["L0_ffn_w_gate_up"], dtype=np.float32)
    wdn_all = np.ascontiguousarray(inputs["L0_ffn_w_down"], dtype=np.float32)
    nc2 = build_stageB2(0)
    in_maps2 = []
    for c in range(NCORE):
        idx_sh = idxa - 4 * c
        in_maps2.append({"x1": x1f, "idx": np.ascontiguousarray(idx_sh), "pos": posa,
                         "counts": np.ascontiguousarray(np.roll(counts, -4 * c, axis=1)),
                         "wgu": wgu_all[4 * c:4 * c + 4], "wdn": wdn_all[4 * c:4 * c + 4],
                         "tokid": tokid, "ident": ident, "ones1": ones1,
                         "iota1024": iota1024})
    print("B2 launch", flush=True)
    res2 = run_bass_kernel_spmd(nc2, in_maps2, core_ids=list(range(NCORE)))
    print("B2 done", flush=True)
    EOf = np.concatenate([r["EO"] for r in res2.results], 0)

    sgu = np.ascontiguousarray(inputs["L0_ffn_shared_w_gate_up"], dtype=np.float32)
    sdn = np.ascontiguousarray(inputs["L0_ffn_shared_w_down"], dtype=np.float32)
    nc3 = build_stageB3()
    in_maps3 = []
    for c in range(NCORE):
        bsel = np.zeros((32, 8), dtype=np.float32); bsel[:, c] = 1.0
        in_maps3.append({"x": np.ascontiguousarray(x1_full[c]), "EOf": EOf,
                         "idx": np.ascontiguousarray(idxa[c * T:(c + 1) * T]),
                         "pos": np.ascontiguousarray(posa[c * T:(c + 1) * T]),
                         "gates": np.ascontiguousarray(gates[c * T:(c + 1) * T]),
                         "counts": counts, "bsel": bsel,
                         "sgu": sgu, "sdn": sdn, "ident": ident, "ones1": ones1,
                         "iota32": iota32})
    print("B3 launch", flush=True)
    res3 = run_bass_kernel_spmd(nc3, in_maps3, core_ids=list(range(NCORE)))
    print("B3 done", flush=True)
    return np.stack([r["y"] for r in res3.results], 0)



def build_stageC():
    P = Prog()
    x = P.inp("x", [T, D])
    WqT = P.inp("WqT", [D, D])
    WkT = P.inp("WkT", [D, D])
    WvT = P.inp("WvT", [D, D])
    WoT = P.inp("WoT", [D, D])
    bot = P.inp("bot", [128, D])
    dbt = P.inp("dbt", [128, 16])
    cmask = P.inp("cmask", [128, 128])
    ident = P.inp("ident", [128, 128])
    y = P.outp("y", [T, D])
    oh = P.nc.dram_tensor("ohs", [2, T, 512], F32, kind="Internal").ap()

    idn = P.sb("idn", [128, 128]); P.dma(idn[:], ident)
    idnb = P.sb("idnb", [128, 128], BF16); P.v('tensor_copy', idnb[:], idn[:])
    cm = P.sb("cm", [128, 128]); P.dma(cm[:], cmask)
    db = P.sb("db", [128, 16]); P.dma(db[:], dbt)
    bos = P.sb("bos", [128, D]); P.dma(bos[:], bot)

    QT = P.sb("QT", [128, 4 * T], BF16)
    KT = P.sb("KT", [128, 4 * T], BF16)
    Vs = P.sb("Vs", [128, 32 * 512], BF16)
    Ss = P.sb("Ss", [128, T])
    Pb = P.sb("Pb", [128, T], BF16)
    xsb = P.sb("xsbC", [128, D])
    xT = P.sb("xTc", [128, 8 * 128], BF16)
    PT = P.sb("PT", [128, 128], BF16)
    OT = P.sb("OTc", [128, 8 * 128], BF16)
    osb = P.sb("osbC", [128, 512])
    ob = P.sb("obC", [128, 1024], BF16)
    fstage = P.sb("fstageC", [128, 512])
    wqb = P.sb("wqb", [128, 8 * 512], BF16)
    wkb = P.sb("wkb", [128, 8 * 512], BF16)
    wvb = P.sb("wvb", [128, 8 * 512], BF16)
    red = P.sb("redC", [128, 4])
    tmp = P.sb("tmpC", [128, D])
    psS = P.ps("psSC", [128, 512])
    ptp = P.ps("ptpC", [128, 128])
    ptpb = P.ps("ptpbC", [128, 128], BF16)
    pO = P.ps("pOC", [128, 64])

    nt = T // 128
    for hh in range(2):
        c0 = hh * 512
        for Wap, dst in ((WqT, wqb), (WkT, wkb), (WvT, wvb)):
            for dc in range(8):
                P.dma(fstage[:], Wap[dc * 128:(dc + 1) * 128, c0:c0 + 512])
                P.v('tensor_copy', dst[:, dc * 512:(dc + 1) * 512], fstage[:])
        for i in range(nt):
            P.dma(xsb[:], x[i * 128:(i + 1) * 128, :])
            for dc in range(8):
                P.tp(ptp[:], xsb[:, dc * 128:(dc + 1) * 128], idn[:])
                P.v('tensor_copy', xT[:, dc * 128:(dc + 1) * 128], ptp[:])
            for wsrc, dst in ((wqb, QT), (wkb, KT)):
                for ic in range(4):
                    for dc in range(8):
                        P.mm(psS[:, 0:128], wsrc[:, dc * 512 + ic * 128: dc * 512 + (ic + 1) * 128],
                             xT[:, dc * 128:(dc + 1) * 128], start=(dc == 0), stop=(dc == 7))
                    P.v('tensor_copy', dst[:, ic * T + i * 128: ic * T + (i + 1) * 128], psS[:, 0:128])
            for dc in range(8):
                P.mm(psS[:], xT[:, dc * 128:(dc + 1) * 128], wvb[:, dc * 512:(dc + 1) * 512],
                     start=(dc == 0), stop=(dc == 7))
            P.v('tensor_copy', Vs[:, i * 512:(i + 1) * 512], psS[:])
        for qt in range(nt):
            kmax = (qt + 1) * 128
            for h in range(8):
                hg = hh * 8 + h
                ic = h // 2
                ro = (h % 2) * 64
                lq = QT[ro:ro + 64, ic * T + qt * 128: ic * T + (qt + 1) * 128]
                for kc0 in range(0, kmax, 512):
                    rem = min(512, kmax - kc0)
                    P.mm(psS[:, 0:rem], lq, KT[ro:ro + 64, ic * T + kc0: ic * T + kc0 + rem],
                         start=True, stop=True)
                    P.act(Ss[:, kc0:kc0 + rem], psS[:, 0:rem], AF.Copy, 0.0, 0.125)
                P.v('tensor_add', Ss[:, qt * 128:(qt + 1) * 128], Ss[:, qt * 128:(qt + 1) * 128], cm[:])
                P.v('reduce_max', red[:, 0:1], Ss[:, 0:kmax], AX)
                P.v('tensor_scalar_mul', red[:, 1:2], red[:, 0:1], -1.0)
                P.act(Ss[:, 0:kmax], Ss[:, 0:kmax], AF.Exp, red[:, 1:2])
                P.v('reduce_sum', red[:, 2:3], Ss[:, 0:kmax], AX)
                P.act(red[:, 3:4], db[:, hg:hg + 1], AF.Exp)
                P.v('tensor_add', red[:, 2:3], red[:, 2:3], red[:, 3:4])
                P.v('reciprocal', red[:, 2:3], red[:, 2:3])
                P.v('tensor_copy', Pb[:, 0:kmax], Ss[:, 0:kmax])
                for kc in range(qt + 1):
                    P.tp(ptpb[:], Pb[:, kc * 128:(kc + 1) * 128], idnb[:])
                    P.v('tensor_copy', PT[:], ptpb[:])
                    P.mm(pO[:], PT[:], Vs[:, kc * 512 + h * 64: kc * 512 + (h + 1) * 64],
                         start=(kc == 0), stop=(kc == qt))
                P.v('tensor_scalar_mul', osb[:, h * 64:(h + 1) * 64], pO[:], red[:, 2:3])
            P.dma(oh[hh, qt * 128:(qt + 1) * 128, :], osb[:])
    wob = QT[:, 0:8 * 1024]
    for dc in range(8):
        P.dma(fstage[:], WoT[dc * 128:(dc + 1) * 128, 0:512])
        P.v('tensor_copy', wob[:, dc * 1024:dc * 1024 + 512], fstage[:])
        P.dma(fstage[:], WoT[dc * 128:(dc + 1) * 128, 512:1024])
        P.v('tensor_copy', wob[:, dc * 1024 + 512:(dc + 1) * 1024], fstage[:])
    for i in range(nt):
        P.dma(osb[:], oh[0, i * 128:(i + 1) * 128, :])
        P.v('tensor_copy', ob[:, 0:512], osb[:])
        P.dma(osb[:], oh[1, i * 128:(i + 1) * 128, :])
        P.v('tensor_copy', ob[:, 512:1024], osb[:])
        for dc in range(8):
            P.tp(ptpb[:], ob[:, dc * 128:(dc + 1) * 128], idnb[:])
            P.v('tensor_copy', OT[:, dc * 128:(dc + 1) * 128], ptpb[:])
        for h2 in range(2):
            for dc in range(8):
                P.mm(psS[:], OT[:, dc * 128:(dc + 1) * 128],
                     wob[:, dc * 1024 + h2 * 512: dc * 1024 + (h2 + 1) * 512],
                     start=(dc == 0), stop=(dc == 7))
            P.v('tensor_copy', tmp[:, h2 * 512:(h2 + 1) * 512], psS[:])
        P.v('tensor_add', tmp[:], tmp[:], bos[:])
        P.dma(xsb[:], x[i * 128:(i + 1) * 128, :])
        P.v('tensor_add', tmp[:], tmp[:], xsb[:])
        emit_ln(P, tmp[:], tmp[:], Ss[:, 0:1024], red[:, 0:2], 1e-5, D)
        P.dma(y[i * 128:(i + 1) * 128, :], tmp[:])
    return P.run()


def run_stageC(inputs, x2_full):
    nc = build_stageC()
    ident = np.eye(128, dtype=np.float32)
    ii, jj = np.meshgrid(np.arange(128), np.arange(128), indexing="ij")
    cmask = np.where(jj > ii, -1e30, 0.0).astype(np.float32)
    db = np.asarray(inputs["L1_attn_denom_bias"], dtype=np.float32).reshape(16)
    dbt = np.ascontiguousarray(np.broadcast_to(db, (128, 16)), dtype=np.float32)
    bot = np.ascontiguousarray(np.broadcast_to(inputs["L1_attn_bo"], (128, D)), dtype=np.float32)
    WqT = np.ascontiguousarray(inputs["L1_attn_Wq"].T, dtype=np.float32)
    WkT = np.ascontiguousarray(inputs["L1_attn_Wk"].T, dtype=np.float32)
    WvT = np.ascontiguousarray(inputs["L1_attn_Wv"].T, dtype=np.float32)
    WoT = np.ascontiguousarray(inputs["L1_attn_Wo"].T, dtype=np.float32)
    in_maps = [{"x": np.ascontiguousarray(x2_full[b], dtype=np.float32),
                "WqT": WqT, "WkT": WkT, "WvT": WvT, "WoT": WoT,
                "bot": bot, "dbt": dbt, "cmask": cmask, "ident": ident}
               for b in range(NCORE)]
    res = run_bass_kernel_spmd(nc, in_maps, core_ids=list(range(NCORE)))
    return np.stack([r["y"] for r in res.results], 0)



def build_stageD():
    P = Prog()
    x = P.inp("x", [T, D])
    wqkv = P.inp("wqkv", [2, D, 1536])
    wo = P.inp("wo", [2, 512, D])
    w1 = P.inp("w1", [2, D, 5460])
    b1t = P.inp("b1t", [2, 128, 5460])
    w2 = P.inp("w2", [2, 2730, D])
    b2t = P.inp("b2t", [2, 128, D])
    lngt = P.inp("lngt", [2, 128, D])
    lnbt = P.inp("lnbt", [2, 128, D])
    qst = P.inp("qst", [2, 128, 512])
    kst = P.inp("kst", [2, 128, 512])
    CSq = P.inp("CSq", [T, 512])
    SSq = P.inp("SSq", [T, 512])
    CSk = P.inp("CSk", [T, 512])
    SSk = P.inp("SSk", [T, 512])
    mown = P.inp("mown", [512, 512])
    mprev = P.inp("mprev", [512, 512])
    ident = P.inp("ident", [128, 128])
    y = P.outp("y", [T, D])
    xcur = P.nc.dram_tensor("xcur", [T, D], F32, kind="Internal").ap()
    oh = P.nc.dram_tensor("ohD", [T, 512], F32, kind="Internal").ap()

    idn = P.sb("idn", [128, 128]); P.dma(idn[:], ident)
    idnb = P.sb("idnbD", [128, 128], BF16); P.v('tensor_copy', idnb[:], idn[:])

    bufX = P.sb("bufX", [128, 32768], BF16)
    QT = bufX[:, 0:16384]
    KT = bufX[:, 16384:32768]
    Vs = P.sb("VsD", [128, 16384], BF16)
    oacc = P.sb("oacc", [128, 4096])
    wqb = P.sb("wqbD", [128, 4096], BF16)
    wkb = P.sb("wkbD", [128, 4096], BF16)
    wvb = P.sb("wvbD", [128, 4096], BF16)
    fst = P.sb("fstD", [128, 1024])
    xsb = P.sb("xsbD", [128, D])
    hsh = P.sb("hshD", [128, D])
    hT = P.sb("hTD", [128, 8 * 128], BF16)
    qsb = P.sb("qsbD", [128, 512])
    rqs = P.sb("rqsD", [128, 512])
    sqt = P.sb("sqtD", [128, 512])
    nrm8 = P.sb("nrm8", [128, 8])
    rot1 = P.sb("rot1", [128, 512])
    rot2 = P.sb("rot2", [128, 512])
    Ss = P.sb("SsD", [128, 1024])
    Pb = P.sb("PbD", [128, 1024], BF16)
    PT = P.sb("PTD", [128, 128], BF16)
    msk = P.sb("mskD", [128, 1024])
    osb = P.sb("osbD", [128, 512])
    ob = P.sb("obD", [128, 512], BF16)
    OT = P.sb("OTD", [128, 4 * 128], BF16)
    mT = P.sb("mTD", [128, 4 * 128], BF16)
    asbuf = P.sb("asbD", [128, 512])
    gsbuf = P.sb("gsbD", [128, 512])
    lng = P.sb("lngD", [128, D])
    lnb = P.sb("lnbD", [128, D])
    qss = P.sb("qssD", [128, 512])
    kss = P.sb("kssD", [128, 512])
    b1as = P.sb("b1as", [128, 512])
    b1gs = P.sb("b1gs", [128, 512])
    b2s = P.sb("b2s", [128, D])
    tmp = P.sb("tmpD", [128, D])
    red = P.sb("redD", [128, 4])
    psS = P.ps("psSD", [128, 512])
    ptp = P.ps("ptpD", [128, 128])
    ptpb = P.ps("ptpbD", [128, 128], BF16)
    pO = P.ps("pOD", [128, 64])
    pA = P.ps("pAD", [128, 512])
    pG = P.ps("pGD", [128, 512])
    pY = P.ps("pYD", [128, 1024])

    nt = T // 128

    for i in range(nt):
        P.dma(xsb[:], x[i * 128:(i + 1) * 128, :])
        P.dma(xcur[i * 128:(i + 1) * 128, :], xsb[:])

    def load_shifted(i):
        P.dma(xsb[:], xcur[i * 128:(i + 1) * 128, :])
        P.v('tensor_copy', hsh[:, 0:512], xsb[:, 0:512])
        if i == 0:
            P.v('memset', hsh[0:1, 512:1024], 0.0)
            P.dma(hsh[1:128, 512:1024], xcur[0:127, 512:1024])
        else:
            P.dma(hsh[:, 512:1024], xcur[i * 128 - 1: i * 128 + 127, 512:1024])

    for l in range(2):
        for dc in range(8):
            P.dma(fst[:, 0:512], wqkv[l, dc * 128:(dc + 1) * 128, 0:512])
            P.v('tensor_copy', wqb[:, dc * 512:(dc + 1) * 512], fst[:, 0:512])
            P.dma(fst[:, 0:512], wqkv[l, dc * 128:(dc + 1) * 128, 512:1024])
            P.v('tensor_copy', wkb[:, dc * 512:(dc + 1) * 512], fst[:, 0:512])
            P.dma(fst[:, 0:512], wqkv[l, dc * 128:(dc + 1) * 128, 1024:1536])
            P.v('tensor_copy', wvb[:, dc * 512:(dc + 1) * 512], fst[:, 0:512])
        P.dma(lng[:], lngt[l]); P.dma(lnb[:], lnbt[l])
        P.dma(qss[:], qst[l]); P.dma(kss[:], kst[l])
        P.dma(b2s[:], b2t[l])

        for i in range(nt):
            load_shifted(i)
            for dc in range(8):
                P.tp(ptp[:], hsh[:, dc * 128:(dc + 1) * 128], idn[:])
                P.v('tensor_copy', hT[:, dc * 128:(dc + 1) * 128], ptp[:])
            for wsrc, dstT, scl, cs, ss in ((wqb, QT, qss, CSq, SSq), (wkb, KT, kss, CSk, SSk)):
                for dc in range(8):
                    P.mm(psS[:], hT[:, dc * 128:(dc + 1) * 128], wsrc[:, dc * 512:(dc + 1) * 512],
                         start=(dc == 0), stop=(dc == 7))
                P.v('tensor_copy', qsb[:], psS[:])
                P.act(sqt[:], qsb[:], AF.Square)
                for h in range(8):
                    P.v('reduce_sum', nrm8[:, h:h + 1], sqt[:, h * 64:(h + 1) * 64], AX)
                P.act(nrm8[:], nrm8[:], AF.Sqrt)
                P.v('tensor_scalar_max', nrm8[:], nrm8[:], 1e-12)
                P.v('reciprocal', nrm8[:], nrm8[:])
                for h in range(8):
                    P.v('tensor_scalar_mul', qsb[:, h * 64:(h + 1) * 64], qsb[:, h * 64:(h + 1) * 64],
                        nrm8[:, h:h + 1])
                P.v('tensor_mul', qsb[:], qsb[:], scl[:])
                for h in range(8):
                    P.v('tensor_scalar_mul', rqs[:, h * 64:h * 64 + 32],
                        qsb[:, h * 64 + 32:(h + 1) * 64], -1.0)
                    P.v('tensor_copy', rqs[:, h * 64 + 32:(h + 1) * 64], qsb[:, h * 64:h * 64 + 32])
                P.dma(rot1[:], cs[i * 128:(i + 1) * 128, :])
                P.dma(rot2[:], ss[i * 128:(i + 1) * 128, :])
                P.v('tensor_mul', qsb[:], qsb[:], rot1[:])
                P.v('tensor_mul', rqs[:], rqs[:], rot2[:])
                P.v('tensor_add', qsb[:], qsb[:], rqs[:])
                for ic in range(4):
                    P.tp(ptp[:], qsb[:, ic * 128:(ic + 1) * 128], idn[:])
                    P.v('tensor_copy', dstT[:, ic * T + i * 128: ic * T + (i + 1) * 128], ptp[:])
            for dc in range(8):
                P.mm(psS[:], hT[:, dc * 128:(dc + 1) * 128], wvb[:, dc * 512:(dc + 1) * 512],
                     start=(dc == 0), stop=(dc == 7))
            P.v('tensor_copy', Vs[:, i * 512:(i + 1) * 512], psS[:])

        for w in range(8):
            for qc in range(4):
                qt = w * 4 + qc
                klen = 512 if w == 0 else 1024
                if w == 0:
                    P.dma(msk[:, 0:512], mown[qc * 128:(qc + 1) * 128, :])
                else:
                    P.dma(msk[:, 0:512], mprev[qc * 128:(qc + 1) * 128, :])
                    P.dma(msk[:, 512:1024], mown[qc * 128:(qc + 1) * 128, :])
                for h in range(8):
                    ic = h // 2
                    ro = (h % 2) * 64
                    lq = QT[ro:ro + 64, ic * T + qt * 128: ic * T + (qt + 1) * 128]
                    kbase = 0 if w == 0 else (w - 1) * 512
                    for j in range(klen // 512):
                        P.mm(psS[:], lq,
                             KT[ro:ro + 64, ic * T + kbase + j * 512: ic * T + kbase + (j + 1) * 512],
                             start=True, stop=True)
                        P.s('copy', Ss[:, j * 512:(j + 1) * 512], psS[:])
                    P.v('tensor_add', Ss[:, 0:klen], Ss[:, 0:klen], msk[:, 0:klen])
                    P.v('reduce_max', red[:, 0:1], Ss[:, 0:klen], AX)
                    P.v('tensor_scalar_mul', red[:, 1:2], red[:, 0:1], -1.0)
                    P.act(Ss[:, 0:klen], Ss[:, 0:klen], AF.Exp, red[:, 1:2])
                    P.v('reduce_sum', red[:, 2:3], Ss[:, 0:klen], AX)
                    P.v('reciprocal', red[:, 2:3], red[:, 2:3])
                    P.v('tensor_copy', Pb[:, 0:klen], Ss[:, 0:klen])
                    nkc = klen // 128
                    for kc in range(nkc):
                        ktt = kbase // 128 + kc
                        P.tp(ptpb[:], Pb[:, kc * 128:(kc + 1) * 128], idnb[:])
                        P.v('tensor_copy', PT[:], ptpb[:])
                        P.mm(pO[:], PT[:], Vs[:, ktt * 512 + h * 64: ktt * 512 + (h + 1) * 64],
                             start=(kc == 0), stop=(kc == nkc - 1))
                    P.v('tensor_scalar_mul', osb[:, h * 64:(h + 1) * 64], pO[:], red[:, 2:3])
                P.dma(oh[qt * 128:(qt + 1) * 128, :], osb[:])

        wob = bufX[:, 0:4096]
        for ic in range(4):
            P.dma(fst[:], wo[l, ic * 128:(ic + 1) * 128, :])
            P.v('tensor_copy', wob[:, ic * 1024:(ic + 1) * 1024], fst[:])
        for i in range(nt):
            P.dma(osb[:], oh[i * 128:(i + 1) * 128, :])
            P.v('tensor_copy', ob[:], osb[:])
            for ic in range(4):
                P.tp(ptpb[:], ob[:, ic * 128:(ic + 1) * 128], idnb[:])
                P.v('tensor_copy', OT[:, ic * 128:(ic + 1) * 128], ptpb[:])
            for h2 in range(2):
                for ic in range(4):
                    P.mm(psS[:], OT[:, ic * 128:(ic + 1) * 128],
                         wob[:, ic * 1024 + h2 * 512: ic * 1024 + (h2 + 1) * 512],
                         start=(ic == 0), stop=(ic == 3))
                P.v('tensor_copy', tmp[:, h2 * 512:(h2 + 1) * 512], psS[:])
            P.dma(xsb[:], xcur[i * 128:(i + 1) * 128, :])
            P.v('tensor_add', tmp[:], tmp[:], xsb[:])
            P.dma(xcur[i * 128:(i + 1) * 128, :], tmp[:])

        w1ab = bufX[:, 0:4096]
        w1gb = bufX[:, 4096:8192]
        w2b = bufX[:, 8192:12288]
        hTg = bufX[:, 12288:16384]
        for grp in range(8):
            tiles = list(range(grp * 4, grp * 4 + 4))
            P.v('memset', oacc[:], 0.0)
            for ti, i in enumerate(tiles):
                load_shifted(i)
                emit_ln(P, hsh[:], hsh[:], tmp[:], red[:, 0:2], 1e-5, D, lng[:], lnb[:])
                for dc in range(8):
                    P.tp(ptp[:], hsh[:, dc * 128:(dc + 1) * 128], idn[:])
                    P.v('tensor_copy', hTg[:, ti * 1024 + dc * 128: ti * 1024 + (dc + 1) * 128], ptp[:])
            for jp in range(6):
                i0 = jp * 512
                wa = min(512, 2730 - i0)
                P.dma(b1as[:, 0:wa], b1t[l, :, i0:i0 + wa])
                P.dma(b1gs[:, 0:wa], b1t[l, :, 2730 + i0:2730 + i0 + wa])
                for dc in range(8):
                    P.dma(fst[:, 0:wa], w1[l, dc * 128:(dc + 1) * 128, i0:i0 + wa])
                    P.v('tensor_copy', w1ab[:, dc * 512:dc * 512 + wa], fst[:, 0:wa])
                    P.dma(fst[:, 0:wa], w1[l, dc * 128:(dc + 1) * 128, 2730 + i0:2730 + i0 + wa])
                    P.v('tensor_copy', w1gb[:, dc * 512:dc * 512 + wa], fst[:, 0:wa])
                nic = (wa + 127) // 128
                for icc in range(nic):
                    rem = min(128, wa - icc * 128)
                    P.dma(fst[0:rem, :], w2[l, i0 + icc * 128: i0 + icc * 128 + rem, :])
                    P.v('tensor_copy', w2b[0:rem, icc * 1024:(icc + 1) * 1024], fst[0:rem, :])
                for ti, i in enumerate(tiles):
                    for dc in range(8):
                        P.mm(pA[:, 0:wa], hTg[:, ti * 1024 + dc * 128: ti * 1024 + (dc + 1) * 128],
                             w1ab[:, dc * 512:dc * 512 + wa], start=(dc == 0), stop=(dc == 7))
                    for dc in range(8):
                        P.mm(pG[:, 0:wa], hTg[:, ti * 1024 + dc * 128: ti * 1024 + (dc + 1) * 128],
                             w1gb[:, dc * 512:dc * 512 + wa], start=(dc == 0), stop=(dc == 7))
                    P.v('tensor_add', asbuf[:, 0:wa], pA[:, 0:wa], b1as[:, 0:wa])
                    P.v('tensor_add', gsbuf[:, 0:wa], pG[:, 0:wa], b1gs[:, 0:wa])
                    P.act(gsbuf[:, 0:wa], gsbuf[:, 0:wa], AF.Gelu)
                    P.v('tensor_mul', asbuf[:, 0:wa], asbuf[:, 0:wa], gsbuf[:, 0:wa])
                    for icc in range(nic):
                        rem = min(128, wa - icc * 128)
                        P.tp(ptp[0:rem, :], asbuf[:, icc * 128:icc * 128 + rem], idn[:])
                        P.v('tensor_copy', mT[0:rem, icc * 128:(icc + 1) * 128], ptp[0:rem, :])
                    for icc in range(nic):
                        rem = min(128, wa - icc * 128)
                        for h2 in range(2):
                            P.mm(pY[:, h2 * 512:(h2 + 1) * 512],
                                 mT[0:rem, icc * 128:icc * 128 + 128],
                                 w2b[0:rem, icc * 1024 + h2 * 512: icc * 1024 + (h2 + 1) * 512],
                                 start=(icc == 0), stop=(icc == nic - 1))
                    P.v('tensor_add', oacc[:, ti * 1024:(ti + 1) * 1024],
                        oacc[:, ti * 1024:(ti + 1) * 1024], pY[:])
            for ti, i in enumerate(tiles):
                P.dma(xsb[:], xcur[i * 128:(i + 1) * 128, :])
                P.v('tensor_add', tmp[:], oacc[:, ti * 1024:(ti + 1) * 1024], b2s[:])
                P.v('tensor_add', tmp[:], tmp[:], xsb[:])
                P.dma(xcur[i * 128:(i + 1) * 128, :], tmp[:])

    for i in range(nt):
        P.dma(xsb[:], x[i * 128:(i + 1) * 128, :])
        P.dma(tmp[:], xcur[i * 128:(i + 1) * 128, :])
        P.v('tensor_add', tmp[:], tmp[:], xsb[:])
        emit_ln(P, tmp[:], tmp[:], hsh[:], red[:, 0:2], 1e-5, D)
        P.dma(y[i * 128:(i + 1) * 128, :], tmp[:])
    return P.run()


def run_stageD(inputs, x3_full):
    nc = build_stageD()
    ident = np.eye(128, dtype=np.float32)
    DIM_HEAD, WINDOW, n = 64, 512, T
    half_idx = np.arange(0, DIM_HEAD, 2, dtype=np.float32)
    inv_freq = 1.0 / 10000.0 ** (half_idx / DIM_HEAD)
    t = np.arange(n, dtype=np.float32)
    freqs = t[:, None] * inv_freq[None, :]
    freqs = np.concatenate([freqs, freqs], axis=-1)
    base = (half_idx + 0.4 * DIM_HEAD) / (1.4 * DIM_HEAD)
    power = (t - n // 2) / float(WINDOW // 2)
    scale = base[None, :] ** power[:, None]
    scale = np.concatenate([scale, scale], axis=-1)
    cos, sin = np.cos(freqs), np.sin(freqs)
    CSq = np.ascontiguousarray(np.tile(cos * scale * 8.0, (1, 8)), dtype=np.float32)
    SSq = np.ascontiguousarray(np.tile(sin * scale * 8.0, (1, 8)), dtype=np.float32)
    CSk = np.ascontiguousarray(np.tile(cos / scale, (1, 8)), dtype=np.float32)
    SSk = np.ascontiguousarray(np.tile(sin / scale, (1, 8)), dtype=np.float32)
    ii, jj = np.meshgrid(np.arange(512), np.arange(512), indexing="ij")
    mown = np.where(jj > ii, -1e30, 0.0).astype(np.float32)
    mprev = np.where(jj >= ii, 0.0, -1e30).astype(np.float32)
    tile2 = lambda a: np.ascontiguousarray(
        np.broadcast_to(a[:, None, :], (2, 128, a.shape[-1])), dtype=np.float32)
    b1t = tile2(inputs["L1_ffn_b1"])
    b2t = tile2(inputs["L1_ffn_b2"])
    lngt = tile2(inputs["L1_ffn_ln_g"])
    lnbt = tile2(inputs["L1_ffn_ln_b"])
    qst = tile2(np.tile(inputs["L1_ffn_q_scale"], (1, 8)))
    kst = tile2(np.tile(inputs["L1_ffn_k_scale"], (1, 8)))
    com = {"wqkv": np.ascontiguousarray(inputs["L1_ffn_wqkv"], dtype=np.float32),
           "wo": np.ascontiguousarray(inputs["L1_ffn_wo"], dtype=np.float32),
           "w1": np.ascontiguousarray(inputs["L1_ffn_w1"], dtype=np.float32),
           "w2": np.ascontiguousarray(inputs["L1_ffn_w2"], dtype=np.float32),
           "b1t": b1t, "b2t": b2t, "lngt": lngt, "lnbt": lnbt, "qst": qst, "kst": kst,
           "CSq": CSq, "SSq": SSq, "CSk": CSk, "SSk": SSk,
           "mown": mown, "mprev": mprev, "ident": ident}
    in_maps = [dict(com, x=np.ascontiguousarray(x3_full[b], dtype=np.float32))
               for b in range(NCORE)]
    res = run_bass_kernel_spmd(nc, in_maps, core_ids=list(range(NCORE)))
    return np.stack([r["y"] for r in res.results], 0)


def kernel(**inputs):
    x = np.ascontiguousarray(np.asarray(inputs["L0_attn_x"]), dtype=np.float32)
    x1 = run_stageA(inputs, x)
    x2 = run_stageB(inputs, x1)
    x3 = run_stageC(inputs, x2)
    y = run_stageD(inputs, x3)
    return y.astype(np.float32)
